# version 13
# baseline (speedup 1.0000x reference)
import numpy as np
import concourse.bass as bass
import concourse.mybir as mybir
from concourse.bass_utils import run_bass_kernel_spmd
from contextlib import ExitStack

F32 = mybir.dt.float32
BF16 = mybir.dt.bfloat16
AF = mybir.ActivationFunctionType
OP = mybir.AluOpType
AX = mybir.AxisListType

B, T, C = 8, 2048, 1024
CC = 8
L = 128
NTC = T // L
NE = 15
I = 512
ICC = 4
EPS = 1e-5


def fl(ap):
    return ap.rearrange("p a b -> p (a b)")


class ES:

    def __init__(self, eng, sem, ms, prefix, self_sync=True):
        self.eng, self.sem, self.ms, self.px = eng, sem, ms, prefix
        self.v = 0
        self.self_sync = self_sync

    def do(self, f, key=None):
        if self.self_sync and self.v > 0:
            self.eng.wait_ge(self.sem, self.v)
        inst = f()
        inst.then_inc(self.sem, 1)
        self.v += 1
        if key is not None:
            self.ms[self.px + ":" + key] = self.v
        return self.v

    def mark(self, inst, key=None):
        inst.then_inc(self.sem, 1)
        self.v += 1
        if key is not None:
            self.ms[self.px + ":" + key] = self.v
        return self.v


def sg_key(idx):
    if idx < 16:
        return f"v:hsh{idx // 4}_{idx % 4}"
    j = idx - 16
    return f"v:h{j // 16}_{(j % 16) // 4}_{j % 4}"


def build_one(ms, rec, stop_after="ffn", taps=(), parts=("id", "sh", "ex")):
    nc = bass.Bass("TRN2", target_bir_lowering=False)

    def MS(key):
        return 0 if rec else ms[key]

    x_d = nc.dram_tensor("x", [C, T], F32, kind="ExternalInput")
    wk_d = nc.dram_tensor("wk", [C, C], F32, kind="ExternalInput")
    wv_d = nc.dram_tensor("wv", [C, C], F32, kind="ExternalInput")
    wr_d = nc.dram_tensor("wr", [C, C], F32, kind="ExternalInput")
    wo_d = nc.dram_tensor("wo", [C, C], F32, kind="ExternalInput")
    par_d = nc.dram_tensor("par", [128, 40], F32, kind="ExternalInput")
    id_d = nc.dram_tensor("idm", [128, 128], F32, kind="ExternalInput")
    gw_d = nc.dram_tensor("gw", [NE, C, I], F32, kind="ExternalInput")
    uw_d = nc.dram_tensor("uw", [NE, C, I], F32, kind="ExternalInput")
    dw_d = nc.dram_tensor("dw", [NE, I, C], F32, kind="ExternalInput")
    cw_d = nc.dram_tensor("cw", [3, C, I], F32, kind="ExternalInput")
    su_d = nc.dram_tensor("su", [C, I], F32, kind="ExternalInput")
    sd_d = nc.dram_tensor("sd", [I, C], F32, kind="ExternalInput")
    rw_d = nc.dram_tensor("rw", [C, 16], F32, kind="ExternalInput")
    rb_d = nc.dram_tensor("rb", [16, 1], F32, kind="ExternalInput")
    comb_d = nc.dram_tensor("comb_dram", [16, T], F32, kind="Internal")
    out_d = nc.dram_tensor("out", [C, T], F32, kind="ExternalOutput")
    tap_d = {}
    for tname in taps:
        shape = {"acc": [C, T], "ln2x": [C, T + 2], "comb": [16, T]}[tname]
        dt = BF16 if tname == "ln2x" else F32
        tap_d[tname] = nc.dram_tensor("tap_" + tname, shape, dt, kind="ExternalOutput")

    xv = x_d.rearrange("(cc p) t -> p cc t", p=128)
    outv = out_d.rearrange("(cc p) t -> p cc t", p=128)

    with ExitStack() as st:
        en = st.enter_context
        acc = en(nc.sbuf_tensor("acc", [128, CC, T], F32))
        par = en(nc.sbuf_tensor("par_sb", [128, 40], F32))
        ew = en(nc.sbuf_tensor("ew", [128, CC], F32))
        ones_b = en(nc.sbuf_tensor("ones_b", [128, 128], BF16))
        id_sb = en(nc.sbuf_tensor("id_sb", [128, 128], F32))

        s_pe = en(nc.semaphore("s_pe"))
        s_dv = en(nc.semaphore("s_dv"))
        s_sc = en(nc.semaphore("s_sc"))
        s_dpar = en(nc.semaphore("s_dpar"))
        s_dwa = en(nc.semaphore("s_dwa"))
        s_dwb = en(nc.semaphore("s_dwb"))
        s_dwf = en(nc.semaphore("s_dwf"))
        s_dsl = [en(nc.semaphore(f"s_dsl{i}")) for i in range(4)]
        s_cb = [en(nc.semaphore(f"s_cb{i}")) for i in range(2)]
        s_dx2 = [en(nc.semaphore(f"s_dx2{i}")) for i in range(2)]
        s_gp = en(nc.semaphore("s_gp"))
        s_dc = en(nc.semaphore("s_dc"))
        s_do = en(nc.semaphore("s_do"))

        U = lambda cc: par[:, cc:cc + 1]
        TMK = lambda cc: par[:, 16 + cc:17 + cc]
        TMV = lambda cc: par[:, 24 + cc:25 + cc]
        TMR = lambda cc: par[:, 32 + cc:33 + cc]

        VS = ES(None, s_dv, ms, "v")
        SS = ES(None, s_sc, ms, "s")
        GS = ES(None, s_gp, ms, "g")
        TS_ = ES(None, s_pe, ms, "t", self_sync=False)

        SCW = 512
        NSC = T // SCW
        with ExitStack() as s1:
            e1 = s1.enter_context
            wsl_a = [e1(nc.sbuf_tensor(f"wsla{i}", [128, CC, C], BF16)) for i in range(2)]
            ewb = e1(nc.sbuf_tensor("ewb", [128, CC, L], F32))
            xb = e1(nc.sbuf_tensor("xb", [128, CC, SCW], BF16))
            xsq = e1(nc.sbuf_tensor("xsq", [128, CC, SCW], BF16))
            ln1x = e1(nc.sbuf_tensor("ln1x", [128, CC, SCW + 1], BF16))
            mx = e1(nc.sbuf_tensor("mx", [128, CC, SCW], BF16))
            mxb = e1(nc.sbuf_tensor("mxb", [128, CC, SCW], BF16))
            k_sb = e1(nc.sbuf_tensor("k_sb", [128, CC, SCW], BF16))
            v_sb = e1(nc.sbuf_tensor("v_sb", [128, CC, SCW], BF16))
            srl = e1(nc.sbuf_tensor("srl", [128, CC, SCW], BF16))
            sry = e1(nc.sbuf_tensor("sry", [128, CC, SCW], BF16))
            m1_sb = e1(nc.sbuf_tensor("m1_sb", [128, SCW], F32))
            tA = e1(nc.sbuf_tensor("tA", [128, SCW], F32))
            e_t = e1(nc.sbuf_tensor("e_t", [128, CC, L], F32))
            a_t = e1(nc.sbuf_tensor("a_t", [128, CC, L], F32))
            av_t = e1(nc.sbuf_tensor("av_t", [128, CC, L], F32))
            den = e1(nc.sbuf_tensor("den", [128, CC, L], F32))
            w1 = e1(nc.sbuf_tensor("w1", [128, CC, L], F32))
            P_ch = e1(nc.sbuf_tensor("P_ch", [128, CC, L + 1], F32))
            Q_ch = e1(nc.sbuf_tensor("Q_ch", [128, CC, L + 1], F32))
            pb = [e1(nc.psum_tensor(f"pb{i}", [128, 512], F32)) for i in range(8)]
            blk = e1(nc.Block())

            @blk.gpsimd
            def _(g):
                GS.eng = g

                def wload(m):
                    wsrc = [wk_d, wv_d, wr_d, wo_d]
                    if m >= 4 * NSC:
                        return
                    if m >= 2:
                        g.wait_ge(s_pe, MS(f"t:MAT{m - 2}"))
                    g.dma_start(wsl_a[m % 2][:],
                                wsrc[m % 4].rearrange("(cc p) d -> p cc d", p=128)
                                ).then_inc(s_dwa if m % 2 == 0 else s_dwb, 16)

                g.dma_start(par[:], par_d[:]).then_inc(s_dpar, 16)
                g.dma_start(id_sb[:], id_d[:]).then_inc(s_dpar, 16)
                wload(0)
                wload(1)
                for s_ in range(NSC):
                    wload(4 * s_ + 2)
                    wload(4 * s_ + 3)
                    wload(4 * s_ + 4)
                    wload(4 * s_ + 5)

            @blk.sync
            def _(sy):
                for s_ in range(NSC):
                    if s_ >= 2:
                        sy.wait_ge(s_dv, MS(f"v:xb{s_ - 2}"))
                    sy.dma_start(acc[:, :, s_ * SCW:(s_ + 1) * SCW],
                                 xv[:, :, s_ * SCW:(s_ + 1) * SCW]).then_inc(s_dx2[s_ % 2], 16)

            @blk.vector
            def _(v):
                VS.eng = v
                v.wait_ge(s_sc, MS("s:ew1"))
                VS.do(lambda: v.tensor_scalar_mul(ew[:], ew[:], -1.0), "negew")
                v.wait_ge(s_sc, MS("s:ew2"))
                VS.do(lambda: v.tensor_copy(ewb[:], ew[:].unsqueeze(2).broadcast_to([128, CC, L])))
                VS.do(lambda: v.memset(ln1x[:, :, 0:1], 0.0))
                VS.do(lambda: v.memset(P_ch[:, :, 0:1], 0.0))
                VS.do(lambda: v.memset(Q_ch[:, :, 0:1], 0.0))
                VS.do(lambda: v.memset(ones_b[:], 1.0), "setup")

                def pre_cast(sp):
                    ssl = slice(sp * SCW, (sp + 1) * SCW)
                    if sp >= 1:
                        v.wait_ge(s_pe, MS(f"t:SQ1_{sp - 1}"))
                    v.wait_ge(s_dx2[sp % 2], 16 * (sp // 2 + 1))
                    VS.do(lambda ssl=ssl: v.tensor_copy(xb[:], acc[:, :, ssl]), f"xb{sp}")
                    VS.do(lambda: v.tensor_tensor(xsq[:], xb[:], xb[:], OP.mult), f"xsq{sp}")

                def pre_maux(sp):
                    v.wait_ge(s_pe, MS(f"t:SQ1_{sp}"))
                    VS.do(lambda: v.tensor_scalar_mul(m1_sb[:], pb[0][:], 1.0 / C))
                    VS.do(lambda: v.tensor_tensor(tA[:], m1_sb[:], m1_sb[:], OP.mult))
                    VS.do(lambda: v.scalar_tensor_tensor(pb[2][:], pb[1][:], 1.0 / C, tA[:], OP.mult, OP.subtract))
                    VS.do(lambda: v.tensor_scalar_add(pb[2][:], pb[2][:], EPS), f"var1_{sp}")

                def pre_normk(sp):
                    v.wait_ge(s_sc, MS(f"s:rstd1_{sp}"))
                    if sp >= 1:
                        VS.do(lambda: v.tensor_copy(ln1x[:, :, 0:1], ln1x[:, :, SCW:SCW + 1]))
                    for l in range(4):
                        lsl = slice(sp * SCW + l * L, sp * SCW + (l + 1) * L)
                        mbc = m1_sb[:, l * L:(l + 1) * L].unsqueeze(1).broadcast_to([128, CC, L])
                        rbc = pb[2][:, l * L:(l + 1) * L].unsqueeze(1).broadcast_to([128, CC, L])
                        VS.do(lambda lsl=lsl, mbc=mbc: v.tensor_tensor(w1[:], acc[:, :, lsl], mbc, OP.subtract))
                        VS.do(lambda l=l, rbc=rbc: v.tensor_tensor(
                            ln1x[:, :, 1 + l * L:1 + (l + 1) * L], w1[:], rbc, OP.mult))
                    VS.ms[f"v:ln1_{sp}"] = VS.v
                    VS.do(lambda: v.tensor_tensor(xsq[:], ln1x[:, :, 1:], ln1x[:, :, 0:SCW], OP.subtract))
                    if sp >= 1:
                        v.wait_ge(s_pe, MS(f"t:MAT{4 * (sp - 1) + 2}"))
                    for cc in range(CC):
                        VS.do(lambda cc=cc: v.scalar_tensor_tensor(
                            mx[:, cc, :], xsq[:, cc, :], TMK(cc), ln1x[:, cc, 0:SCW], OP.mult, OP.add))
                    VS.ms[f"v:mixk{sp}"] = VS.v

                def pre_mixv(sp):
                    for cc in range(CC):
                        VS.do(lambda cc=cc: v.scalar_tensor_tensor(
                            mxb[:, cc, :], xsq[:, cc, :], TMV(cc), ln1x[:, cc, 0:SCW], OP.mult, OP.add))
                    VS.ms[f"v:mixv{sp}"] = VS.v

                pre_cast(0)
                pre_maux(0)
                pre_normk(0)
                pre_mixv(0)
                for s_ in range(NSC):
                    ssl = slice(s_ * SCW, (s_ + 1) * SCW)
                    v.wait_ge(s_pe, MS(f"t:MAT{4 * s_ + 0}"))
                    for cc in range(CC):
                        VS.do(lambda cc=cc: v.scalar_tensor_tensor(
                            mx[:, cc, :], xsq[:, cc, :], TMR(cc), ln1x[:, cc, 0:SCW], OP.mult, OP.add))
                    VS.ms[f"v:mixr{s_}"] = VS.v
                    if s_ >= 1:
                        psl = slice((s_ - 1) * SCW, s_ * SCW)
                        v.wait_ge(s_pe, MS(f"t:MAT{4 * (s_ - 1) + 3}"))
                        for od in range(CC):
                            VS.do(lambda od=od, psl=psl: v.tensor_tensor(
                                acc[:, od, psl], acc[:, od, psl], pb[od][:], OP.add))
                        VS.ms[f"v:accev{s_ - 1}"] = VS.v
                    if s_ + 1 < NSC:
                        pre_cast(s_ + 1)
                    for l in range(4):
                        ksl = slice(l * L, (l + 1) * L)
                        v.wait_ge(s_sc, MS(f"s:ea{s_}_{l}"))
                        if l == 0:
                            v.wait_ge(s_sc, MS(f"s:vevac{s_}"))
                        VS.do(lambda ksl=ksl: v.tensor_tensor(w1[:], e_t[:], v_sb[:, :, ksl], OP.mult))
                        VS.do(lambda ksl=ksl: v.tensor_tensor(av_t[:], a_t[:], v_sb[:, :, ksl], OP.mult))
                        if not (s_ == 0 and l == 0):
                            VS.do(lambda: v.tensor_copy(P_ch[:, :, 0:1], P_ch[:, :, L:L + 1]))
                            VS.do(lambda: v.tensor_copy(Q_ch[:, :, 0:1], Q_ch[:, :, L:L + 1]))
                        for cc in range(CC):
                            VS.do(lambda cc=cc: v.tensor_tensor_scan(
                                P_ch[:, cc, 1:], ewb[:, cc, :], w1[:, cc, :],
                                P_ch[:, cc, 0:1], OP.mult, OP.add))
                        for cc in range(CC):
                            VS.do(lambda cc=cc: v.tensor_tensor_scan(
                                Q_ch[:, cc, 1:], ewb[:, cc, :], e_t[:, cc, :],
                                Q_ch[:, cc, 0:1], OP.mult, OP.add))
                        VS.do(lambda: v.tensor_tensor(den[:], Q_ch[:, :, 0:L], a_t[:], OP.add), f"den{s_}_{l}")
                        VS.do(lambda: v.tensor_tensor(w1[:], P_ch[:, :, 0:L], av_t[:], OP.add))
                        v.wait_ge(s_sc, MS(f"s:rden{s_}_{l}"))
                        VS.do(lambda: v.tensor_tensor(w1[:], w1[:], e_t[:], OP.mult))
                        VS.do(lambda ksl=ksl: v.tensor_tensor(
                            sry[:, :, ksl], srl[:, :, ksl], w1[:], OP.mult), f"sry{s_}_{l}")
                        if s_ + 1 < NSC:
                            if l == 0:
                                pre_maux(s_ + 1)
                            elif l == 1:
                                pre_normk(s_ + 1)
                            elif l == 2:
                                pre_mixv(s_ + 1)
                sN = NSC - 1
                v.wait_ge(s_pe, MS(f"t:MAT{4 * sN + 3}"))
                for od in range(CC):
                    VS.do(lambda od=od: v.tensor_tensor(
                        acc[:, od, sN * SCW:(sN + 1) * SCW],
                        acc[:, od, sN * SCW:(sN + 1) * SCW], pb[od][:], OP.add))
                VS.ms[f"v:accev{sN}"] = VS.v

            @blk.scalar
            def _(s):
                SS.eng = s
                s.wait_ge(s_dpar, 32)
                SS.do(lambda: s.activation(ew[:], par[:, 8:16], AF.Exp), "ew1")
                s.wait_ge(s_dv, MS("v:negew"))
                SS.do(lambda: s.activation(ew[:], ew[:], AF.Exp), "ew2")

                def emit_rstd(sp):
                    s.wait_ge(s_dv, MS(f"v:var1_{sp}"))
                    SS.do(lambda: s.activation(pb[2][:], pb[2][:], AF.Ln))
                    SS.do(lambda: s.activation(pb[2][:], pb[2][:], AF.Exp, scale=-0.5), f"rstd1_{sp}")

                def emit_ea(s_, l):
                    ksl = slice(l * L, (l + 1) * L)
                    if l >= 1:
                        s.wait_ge(s_dv, MS(f"v:sry{s_}_{l - 1}"))
                    elif s_ >= 1:
                        s.wait_ge(s_dv, MS(f"v:sry{s_ - 1}_3"))
                    SS.do(lambda ksl=ksl: s.activation(e_t[:], k_sb[:, :, ksl], AF.Exp))
                    for cc in range(CC):
                        SS.do(lambda cc=cc, ksl=ksl: s.activation(
                            a_t[:, cc, :], k_sb[:, cc, ksl], AF.Exp, bias=U(cc)))
                    SS.ms[f"s:ea{s_}_{l}"] = SS.v

                def emit_rden(s_, l):
                    s.wait_ge(s_dv, MS(f"v:den{s_}_{l}"))
                    SS.do(lambda: s.activation(fl(den[:]), fl(den[:]), AF.Ln))
                    SS.do(lambda: s.activation(fl(e_t[:]), fl(den[:]), AF.Exp, scale=-1.0),
                          f"rden{s_}_{l}")

                emit_rstd(0)
                for s_ in range(NSC):
                    s.wait_ge(s_pe, MS(f"t:MAT{4 * s_ + 0}"))
                    for od in range(CC):
                        SS.do(lambda od=od: s.copy(k_sb[:, od, :], pb[od][:]))
                    SS.ms[f"s:kevac{s_}"] = SS.v
                    emit_ea(s_, 0)
                    s.wait_ge(s_pe, MS(f"t:MAT{4 * s_ + 1}"))
                    for od in range(CC):
                        SS.do(lambda od=od: s.copy(v_sb[:, od, :], pb[od][:]))
                    SS.ms[f"s:vevac{s_}"] = SS.v
                    s.wait_ge(s_pe, MS(f"t:MAT{4 * s_ + 2}"))
                    for od in range(CC):
                        SS.do(lambda od=od: s.activation(pb[od][:], pb[od][:], AF.Exp, scale=-1.0))
                        SS.do(lambda od=od: s.activation(pb[od][:], pb[od][:], AF.Ln, bias=1.0))
                        SS.do(lambda od=od: s.activation(srl[:, od, :], pb[od][:], AF.Exp, scale=-1.0),
                              f"sigread_{s_}" if od == CC - 1 else None)
                    SS.ms[f"s:sig{s_}"] = SS.v
                    emit_rden(s_, 0)
                    emit_ea(s_, 1)
                    emit_rden(s_, 1)
                    if s_ + 1 < NSC:
                        emit_rstd(s_ + 1)
                    for l in range(2, 4):
                        emit_ea(s_, l)
                        emit_rden(s_, l)

            @blk.tensor
            def _(t):
                TS_.eng = t
                t.wait_ge(s_dv, MS("v:setup"))

                def emit_stats(sp):
                    t.wait_ge(s_dv, MS(f"v:xsq{sp}"))
                    if sp >= 1:
                        t.wait_ge(s_sc, MS(f"s:sigread_{sp - 1}"))
                    for cc in range(CC):
                        inst = t.matmul(pb[0][:], ones_b[:], xb[:, cc, :], start=(cc == 0), stop=(cc == CC - 1))
                    TS_.mark(inst, f"S1_{sp}")
                    for cc in range(CC):
                        inst = t.matmul(pb[1][:], ones_b[:], xsq[:, cc, :], start=(cc == 0), stop=(cc == CC - 1))
                    TS_.mark(inst, f"SQ1_{sp}")

                def emit_mat(m, rhs_t):
                    for od in range(CC):
                        for kc in range(CC):
                            inst = t.matmul(pb[od][:],
                                            wsl_a[m % 2][:, kc, od * 128:(od + 1) * 128],
                                            rhs_t[:, kc, :],
                                            start=(kc == 0), stop=(kc == CC - 1))
                    TS_.mark(inst, f"MAT{m}")

                emit_stats(0)
                t.wait_ge(s_dwa, 16)
                t.wait_ge(s_dv, MS("v:mixk0"))
                emit_mat(0, mx)
                for s_ in range(NSC):
                    m = 4 * s_ + 1
                    t.wait_ge(s_dwa if m % 2 == 0 else s_dwb, 16 * (m // 2 + 1))
                    t.wait_ge(s_dv, MS(f"v:mixv{s_}"))
                    t.wait_ge(s_sc, MS(f"s:kevac{s_}"))
                    if s_ >= 1:
                        t.wait_ge(s_dv, MS(f"v:accev{s_ - 1}"))
                    emit_mat(m, mxb)
                    m = 4 * s_ + 2
                    t.wait_ge(s_dwa if m % 2 == 0 else s_dwb, 16 * (m // 2 + 1))
                    t.wait_ge(s_dv, MS(f"v:mixr{s_}"))
                    t.wait_ge(s_sc, MS(f"s:vevac{s_}"))
                    emit_mat(m, mx)
                    if s_ + 1 < NSC:
                        emit_stats(s_ + 1)
                        m = 4 * (s_ + 1)
                        t.wait_ge(s_dwa if m % 2 == 0 else s_dwb, 16 * (m // 2 + 1))
                        t.wait_ge(s_dv, MS(f"v:mixk{s_ + 1}"))
                        t.wait_ge(s_sc, MS(f"s:sigread_{s_}"))
                        emit_mat(m, mx)
                    m = 4 * s_ + 3
                    t.wait_ge(s_dwa if m % 2 == 0 else s_dwb, 16 * (m // 2 + 1))
                    t.wait_ge(s_dv, MS(f"v:sry{s_}_3"))
                    if s_ + 1 < NSC:
                        t.wait_ge(s_sc, MS(f"s:kevac{s_ + 1}"))
                    emit_mat(m, sry)

        if stop_after == "attn":
            with nc.Block() as blk9:
                @blk9.sync
                def _(sy):
                    n = 1
                    sy.dma_start(outv, acc[:]).then_inc(s_do, 16)
                    if "acc" in tap_d:
                        sy.dma_start(tap_d["acc"].rearrange("(cc p) t -> p cc t", p=128),
                                     acc[:]).then_inc(s_do, 16)
                        n += 1
                    sy.wait_ge(s_do, 16 * n)
            return nc

        with ExitStack() as s2:
            e2 = s2.enter_context
            ln2x = e2(nc.sbuf_tensor("ln2x", [128, CC, T + 2], BF16))
            cwS = e2(nc.sbuf_tensor("cw_sb", [128, 24, I], BF16))
            suS = e2(nc.sbuf_tensor("su_sb", [128, CC, I], BF16))
            sdS = e2(nc.sbuf_tensor("sd_sb", [128, ICC, C], BF16))
            rwS = e2(nc.sbuf_tensor("rw_sb", [128, CC, 16], BF16))
            rbS = e2(nc.sbuf_tensor("rb_sb", [16, 1], F32))
            wsl = [e2(nc.sbuf_tensor(f"wsl{i}", [128, CC, I], BF16)) for i in range(4)]

            def slot_ap(m):
                s = wsl[m % 4]
                if m % 3 == 2:
                    return s[:].rearrange("p a b -> p (a b)").rearrange("p (a b) -> p a b", b=C)
                return s[:]

            with ExitStack() as sb2:
                eb2 = sb2.enter_context
                accb = eb2(nc.sbuf_tensor("accb", [128, CC, 512], BF16))
                xsqb = eb2(nc.sbuf_tensor("xsqb", [128, CC, 512], BF16))
                m_sb = eb2(nc.sbuf_tensor("m_sb", [128, 512], F32))
                vr_sb = eb2(nc.sbuf_tensor("vr_sb", [128, 512], F32))
                sgl = eb2(nc.sbuf_tensor("sgl", [128, 512], F32))
                pc = [eb2(nc.psum_tensor(f"pc{i}", [128, 512], F32)) for i in range(2)]
                blk2 = eb2(nc.Block())

                @blk2.gpsimd
                def _(g):
                    g.dma_start(cwS[:], cw_d.rearrange("k (cc p) o -> p (k cc) o", p=128)).then_inc(s_dwf, 16)
                    g.dma_start(suS[:], su_d.rearrange("(cc p) o -> p cc o", p=128)).then_inc(s_dwf, 16)
                    g.dma_start(sdS[:], sd_d.rearrange("(cc p) o -> p cc o", p=128)).then_inc(s_dwf, 16)
                    g.dma_start(rwS[:], rw_d.rearrange("(cc p) o -> p cc o", p=128)).then_inc(s_dwf, 16)
                    g.dma_start(rbS[:], rb_d[:]).then_inc(s_dwf, 16)
                    for m in range(4):
                        e, j = m // 3, m % 3
                        src = (gw_d, uw_d, dw_d)[j]
                        view = src[e].rearrange("(cc p) o -> p cc o", p=128)
                        g.dma_start(slot_ap(m), view).then_inc(s_dsl[m % 4], 16)

                @blk2.vector
                def _(v):
                    VS.eng = v
                    VS.do(lambda: v.tensor_copy(accb[:], acc[:, :, 0:512]), "cast0")
                    VS.do(lambda: v.tensor_tensor(xsqb[:], accb[:], accb[:], OP.mult), "cast2_0")
                    for g_ in range(4):
                        gs = slice(g_ * 512, (g_ + 1) * 512)
                        v.wait_ge(s_pe, MS(f"t:SQ2_{g_}"))
                        VS.do(lambda: v.tensor_scalar_mul(m_sb[:], pc[0][:], 1.0 / C))
                        VS.do(lambda: v.tensor_tensor(vr_sb[:], m_sb[:], m_sb[:], OP.mult))
                        VS.do(lambda: v.scalar_tensor_tensor(vr_sb[:], pc[1][:], 1.0 / C, vr_sb[:],
                                                             OP.mult, OP.subtract))
                        VS.do(lambda: v.tensor_scalar_add(vr_sb[:], vr_sb[:], EPS), f"var2_{g_}")
                        if g_ + 1 < 4:
                            ns = slice((g_ + 1) * 512, (g_ + 2) * 512)
                            VS.do(lambda ns=ns: v.tensor_copy(accb[:], acc[:, :, ns]), f"cast{g_ + 1}")
                            VS.do(lambda: v.tensor_tensor(xsqb[:], accb[:], accb[:], OP.mult), f"cast2_{g_ + 1}")
                        v.wait_ge(s_sc, MS(f"s:rstd2_{g_}"))
                        for cc in range(CC):
                            VS.do(lambda cc=cc, gs=gs: v.tensor_tensor(sgl[:], acc[:, cc, gs], m_sb[:], OP.subtract))
                            VS.do(lambda cc=cc, g_=g_: v.tensor_tensor(
                                ln2x[:, cc, 2 + g_ * 512: 2 + (g_ + 1) * 512], sgl[:], vr_sb[:], OP.mult))
                        VS.ms[f"v:norm{g_}"] = VS.v
                    VS.do(lambda: v.memset(ln2x[:, :, 0:2], 0.0), "ln2pad")

                @blk2.scalar
                def _(s):
                    SS.eng = s
                    for g_ in range(4):
                        s.wait_ge(s_dv, MS(f"v:var2_{g_}"))
                        SS.do(lambda: s.activation(vr_sb[:], vr_sb[:], AF.Ln))
                        SS.do(lambda: s.activation(vr_sb[:], vr_sb[:], AF.Exp, scale=-0.5), f"rstd2_{g_}")

                @blk2.tensor
                def _(t):
                    for g_ in range(4):
                        t.wait_ge(s_dv, MS(f"v:cast2_{g_}"))
                        for cc in range(CC):
                            inst = t.matmul(pc[0][:], ones_b[:], accb[:, cc, :],
                                            start=(cc == 0), stop=(cc == CC - 1))
                        TS_.mark(inst, f"S2_{g_}")
                        for cc in range(CC):
                            inst = t.matmul(pc[1][:], ones_b[:], xsqb[:, cc, :],
                                            start=(cc == 0), stop=(cc == CC - 1))
                        TS_.mark(inst, f"SQ2_{g_}")

            if stop_after == "ln2":
                with nc.Block() as blk9:
                    @blk9.sync
                    def _(sy):
                        n = 1
                        sy.dma_start(outv, acc[:]).then_inc(s_do, 16)
                        if "ln2x" in tap_d:
                            sy.dma_start(tap_d["ln2x"].rearrange("(cc p) t -> p cc t", p=128),
                                         ln2x[:]).then_inc(s_do, 16)
                            n += 1
                        sy.wait_ge(s_do, 16 * n)
                return nc

            with ExitStack() as sb3:
                eb3 = sb3.enter_context
                lg = eb3(nc.sbuf_tensor("lg", [16, T], F32))
                ptm = eb3(nc.sbuf_tensor("ptm", [128, 16, 16], F32))
                ptm2 = eb3(nc.sbuf_tensor("ptm2", [128, 16, 16], F32))
                psv = eb3(nc.sbuf_tensor("psv", [128, 16, 16], F32))
                flsh = eb3(nc.sbuf_tensor("flsh", [128, 16], F32))
                ctm = eb3(nc.sbuf_tensor("ctm", [128, 16, 16], F32))
                cfm = eb3(nc.sbuf_tensor("cfm", [16, T], F32))
                aux = eb3(nc.sbuf_tensor("aux", [128, 16, 4], F32))
                pc3 = [eb3(nc.psum_tensor(f"pc3{i}", [128, 512], F32)) for i in range(8)]
                blk3 = eb3(nc.Block())

                @blk3.tensor
                def _(t):
                    t.wait_ge(s_dwf, 16 * 5)
                    for tc in range(4):
                        for kc in range(CC):
                            inst = t.matmul(pc3[4 + tc][0:16, :], rwS[:, kc, :],
                                            ln2x[:, kc, 2 + tc * 512: 2 + (tc + 1) * 512],
                                            start=(kc == 0), stop=(kc == CC - 1))
                        TS_.mark(inst, f"lg{tc}")
                    t.wait_ge(s_dv, MS("v:lgev"))
                    for i in range(16):
                        inst = t.matmul(pc3[0][:, i * 16:(i + 1) * 16],
                                        lg[0:16, i * 128:(i + 1) * 128],
                                        id_sb[0:16, 0:16], start=True, stop=True)
                        TS_.mark(inst, f"fwdT{i}" if i < 15 else "fwdT")
                    t.wait_ge(s_dv, MS("v:smax"))
                    for i in range(16):
                        inst = t.matmul(pc3[4 + i // 4][0:16, (i % 4) * 128:(i % 4 + 1) * 128],
                                        fl(ctm)[:, i * 16:(i + 1) * 16],
                                        id_sb[:, 0:128], start=True, stop=True)
                        TS_.mark(inst, f"backT{i}" if i < 15 else "backT")

                @blk3.vector
                def _(v):
                    VS.eng = v
                    v.wait_ge(s_pe, MS("t:lg3"))
                    for tc in range(4):
                        VS.do(lambda tc=tc: v.tensor_scalar_add(
                            lg[0:16, tc * 512:(tc + 1) * 512], pc3[4 + tc][0:16, :], rbS[:]))
                    VS.ms["v:lgev"] = VS.v
                    v.wait_ge(s_pe, MS("t:fwdT"))
                    VS.do(lambda: v.tensor_copy(fl(ptm), pc3[0][:, 0:256]))
                    VS.do(lambda: v.reduce_max(aux[:, :, 0:1], ptm[:], axis=AX.X))
                    VS.do(lambda: v.tensor_tensor(psv[:], ptm[:],
                                                  aux[:, :, 0:1].broadcast_to([128, 16, 16]), OP.subtract),
                          "presoft")
                    v.wait_ge(s_sc, MS("s:sexp"))
                    VS.do(lambda: v.reduce_sum(aux[:, :, 1:2], ptm2[:], axis=AX.X))
                    VS.do(lambda: v.reciprocal(fl(aux[:, :, 2:3]), fl(aux[:, :, 1:2])))
                    VS.do(lambda: v.tensor_tensor(ptm[:], ptm2[:], aux[:, :, 2:3].broadcast_to([128, 16, 16]), OP.mult))
                    VS.do(lambda: v.reduce_max(aux[:, :, 0:1], ptm[:], axis=AX.X))
                    VS.do(lambda: v.tensor_tensor(ctm[:], ptm[:], aux[:, :, 0:1].broadcast_to([128, 16, 16]), OP.is_equal))
                    VS.do(lambda: v.scalar_tensor_tensor(psv[:], ctm[:], -1e30, ptm[:], OP.mult, OP.add))
                    VS.do(lambda: v.reduce_max(aux[:, :, 1:2], psv[:], axis=AX.X))
                    VS.do(lambda: v.tensor_tensor(ctm[:], ptm[:], aux[:, :, 1:2].broadcast_to([128, 16, 16]), OP.is_ge))
                    VS.do(lambda: v.tensor_tensor(aux[:, :, 3:4], aux[:, :, 0:1], aux[:, :, 1:2], OP.add))
                    VS.do(lambda: v.tensor_scalar_add(aux[:, :, 0:1], aux[:, :, 3:4], 1e-6))
                    VS.do(lambda: v.reciprocal(fl(aux[:, :, 3:4]), fl(aux[:, :, 0:1])))
                    VS.do(lambda: v.tensor_tensor(psv[:], ctm[:], ptm[:], OP.mult))
                    VS.do(lambda: v.tensor_tensor(ctm[:], psv[:], aux[:, :, 3:4].broadcast_to([128, 16, 16]), OP.mult),
                          "smax")
                    v.wait_ge(s_pe, MS("t:backT"))
                    for i in range(4):
                        VS.do(lambda i=i: v.tensor_copy(cfm[0:16, i * 512:(i + 1) * 512], pc3[4 + i][0:16, :]))
                    VS.ms["v:cfmev"] = VS.v

                @blk3.scalar
                def _(s):
                    SS.eng = s
                    s.wait_ge(s_dv, MS("v:presoft"))
                    SS.do(lambda: s.activation(fl(ptm2), fl(psv), AF.Exp), "sexp")

                @blk3.sync
                def _(sy):
                    sy.wait_ge(s_dv, MS("v:cfmev"))
                    sy.dma_start(comb_d[:], cfm[:]).then_inc(s_dc, 16)
                    sy.wait_ge(s_dc, 16)
                    if "comb" in tap_d:
                        sy.dma_start(tap_d["comb"][:], cfm[:]).then_inc(s_do, 16)
                    if "ln2x" in tap_d:
                        sy.dma_start(tap_d["ln2x"].rearrange("(cc p) t -> p cc t", p=128),
                                     ln2x[:]).then_inc(s_do, 16)

            if stop_after == "router":
                with nc.Block() as blk9:
                    @blk9.sync
                    def _(sy):
                        sy.dma_start(outv, acc[:]).then_inc(s_do, 16)
                        sy.wait_ge(s_do, 16 * (1 + ("comb" in tap_d) + ("ln2x" in tap_d)))
                return nc

            with ExitStack() as sb4:
                eb4 = sb4.enter_context
                h = eb4(nc.sbuf_tensor("h", [128, ICC, T], BF16))
                cb = [eb4(nc.sbuf_tensor(f"cb{i}", [128, 4, 512], BF16)) for i in range(2)]
                sg = [eb4(nc.sbuf_tensor(f"sg{i}", [128, 512], F32)) for i in range(2)]
                dmy = eb4(nc.sbuf_tensor("dmy", [128, 512], F32))
                pc4 = [eb4(nc.psum_tensor(f"pc4{i}", [128, 512], F32)) for i in range(8)]
                blk4 = eb4(nc.Block())

                gp_seq = ["comb0", "comb1"]
                for m in range(4, 3 * NE):
                    e_of = m // 3
                    if m % 3 == 0 and e_of >= 2:
                        gp_seq.append(f"comb{e_of}")
                    gp_seq.append(f"m{m}")

                def cb_count(e):
                    if e == 15:
                        return 1
                    return (e // 2 + 1) if e % 2 == 0 else ((e + 1) // 2 + 1)

                def consumed_key(m):
                    e, j = m // 3, m % 3
                    return (f"t:gate{e}_3", f"t:up{e}_3", f"t:down{e}_7")[j]

                @blk4.gpsimd
                def _(g):
                    GS.eng = g
                    g.dma_start(fl(cb[1][:]),
                                comb_d[15][None, :].broadcast_to([128, T])).then_inc(s_cb[1], 16)
                    g.wait_ge(s_cb[1], 16)
                    for tc in range(4):
                        cbc = cb[1][:, tc, :].unsqueeze(1).broadcast_to([128, CC, 512])
                        hf = fl(h[:]).bitcast(F32).rearrange("p (cc t) -> p cc t", t=512)
                        GS.do(lambda tc=tc, cbc=cbc, hf=hf: g.tensor_tensor(
                            hf[:], ln2x[:, :, 2 + tc * 512: 2 + (tc + 1) * 512], cbc, OP.mult))
                        dst = acc[:, :, tc * 512:(tc + 1) * 512] if "id" in parts else hf[:]
                        GS.do(lambda tc=tc, dst=dst, hf=hf: g.tensor_tensor(
                            dst, acc[:, :, tc * 512:(tc + 1) * 512], hf[:], OP.add), f"id{tc}")
                    for nm in gp_seq:
                        if nm.startswith("comb"):
                            e = 15 if nm == "comb15" else int(nm[4:])
                            buf = cb[1] if nm == "comb15" else cb[e % 2]
                            if nm == "comb1":
                                pass
                            elif nm != "comb15" and nm != "comb0" and e >= 2:
                                g.wait_ge(s_dv, MS(f"v:h{e - 2}_3_3"))
                            g.dma_start(fl(buf[:]),
                                        comb_d[e][None, :].broadcast_to([128, T])
                                        ).then_inc(s_cb[1 if nm == "comb15" else e % 2], 16)
                        else:
                            m = int(nm[1:])
                            e, j = m // 3, m % 3
                            g.wait_ge(s_pe, MS(consumed_key(m - 4)))
                            src = (gw_d, uw_d, dw_d)[j]
                            view = src[e].rearrange("(cc p) o -> p cc o", p=128)
                            g.dma_start(slot_ap(m), view).then_inc(s_dsl[m % 4], 16)

                @blk4.tensor
                def _(t):
                    t.wait_ge(s_dwf, 16 * 5)
                    for tc in range(4):
                        t0 = 2 + tc * 512
                        if tc >= 1:
                            t.wait_ge(s_dv, MS(f"v:dash{tc - 1}_7"))
                        for ocI in range(ICC):
                            for kk in range(3):
                                for kc in range(CC):
                                    inst = t.matmul(pc4[ocI][:],
                                                    cwS[:, kk * 8 + kc, ocI * 128:(ocI + 1) * 128],
                                                    ln2x[:, kc, t0 - 2 + kk: t0 - 2 + kk + 512],
                                                    start=(kk == 0 and kc == 0),
                                                    stop=(kk == 2 and kc == CC - 1))
                        TS_.mark(inst, f"conv{tc}")
                        for ocI in range(ICC):
                            for kc in range(CC):
                                inst = t.matmul(pc4[4 + ocI][:],
                                                suS[:, kc, ocI * 128:(ocI + 1) * 128],
                                                ln2x[:, kc, t0:t0 + 512],
                                                start=(kc == 0), stop=(kc == CC - 1))
                        TS_.mark(inst, f"upsh{tc}")
                        for oc in range(CC):
                            t.wait_ge(s_sc, MS(f"s:sish{tc}_3"))
                            t.wait_ge(s_dv, MS(f"v:hsh{tc}_3"))
                            if oc >= 4:
                                t.wait_ge(s_dv, MS(f"v:dash{tc}_{oc - 4}"))
                            for kcI in range(ICC):
                                inst = t.matmul(pc4[oc % 4][:],
                                                sdS[:, kcI, oc * 128:(oc + 1) * 128],
                                                h[:, kcI, tc * 512:(tc + 1) * 512],
                                                start=(kcI == 0), stop=(kcI == ICC - 1))
                            TS_.mark(inst, f"dnsh{tc}_{oc}")
                    for e in range(NE):
                        for ocI in range(ICC):
                            t.wait_ge(s_dsl[(3 * e) % 4], 16 * ((3 * e) // 4 + 1))
                            if e == 0 and ocI == 0:
                                t.wait_ge(s_dv, MS("v:dash3_7"))
                            elif ocI == 0:
                                t.wait_ge(s_dv, MS(f"v:dadd{e - 1}_6"))
                            else:
                                t.wait_ge(s_sc, MS(f"s:silu{e}_{ocI - 1}_3"))
                            for kc in range(CC):
                                for tc in range(4):
                                    inst = t.matmul(pc4[tc][:],
                                                    slot_ap(3 * e)[:, kc, ocI * 128:(ocI + 1) * 128],
                                                    ln2x[:, kc, 2 + tc * 512: 2 + (tc + 1) * 512],
                                                    start=(kc == 0), stop=(kc == CC - 1))
                            TS_.mark(inst, f"gate{e}_{ocI}")
                            t.wait_ge(s_dsl[(3 * e + 1) % 4], 16 * ((3 * e + 1) // 4 + 1))
                            if e == 0 and ocI == 0:
                                pass
                            elif ocI == 0:
                                t.wait_ge(s_dv, MS(f"v:dadd{e - 1}_7"))
                            else:
                                t.wait_ge(s_dv, MS(f"v:h{e}_{ocI - 1}_3"))
                            for kc in range(CC):
                                for tc in range(4):
                                    inst = t.matmul(pc4[4 + tc][:],
                                                    slot_ap(3 * e + 1)[:, kc, ocI * 128:(ocI + 1) * 128],
                                                    ln2x[:, kc, 2 + tc * 512: 2 + (tc + 1) * 512],
                                                    start=(kc == 0), stop=(kc == CC - 1))
                            TS_.mark(inst, f"up{e}_{ocI}")
                        t.wait_ge(s_dsl[(3 * e + 2) % 4], 16 * ((3 * e + 2) // 4 + 1))
                        t.wait_ge(s_dv, MS(f"v:h{e}_3_3"))
                        for oc in range(CC):
                            if oc >= 2:
                                t.wait_ge(s_dv, MS(f"v:dadd{e}_{oc - 2}"))
                            for kcI in range(ICC):
                                for tc in range(4):
                                    inst = t.matmul(pc4[(oc % 2) * 4 + tc][:],
                                                    slot_ap(3 * e + 2)[:, kcI, oc * 128:(oc + 1) * 128],
                                                    h[:, kcI, tc * 512:(tc + 1) * 512],
                                                    start=(kcI == 0), stop=(kcI == ICC - 1))
                            TS_.mark(inst, f"down{e}_{oc}")

                @blk4.vector
                def _(v):
                    VS.eng = v
                    v.wait_ge(s_gp, MS("g:id3"))
                    for tc in range(4):
                        for ocI in range(ICC):
                            idx = 4 * tc + ocI
                            sl = sg[idx % 2]
                            v.wait_ge(s_sc, MS(f"s:sish{tc}_{ocI}"))
                            if ocI == 0:
                                v.wait_ge(s_pe, MS(f"t:upsh{tc}"))
                            VS.do(lambda sl=sl, ocI=ocI, tc=tc: v.tensor_tensor(
                                h[:, ocI, tc * 512:(tc + 1) * 512], sl[:], pc4[4 + ocI][:],
                                OP.mult), f"hsh{tc}_{ocI}")
                        for oc in range(CC):
                            v.wait_ge(s_pe, MS(f"t:dnsh{tc}_{oc}"))
                            ac = acc[:, oc, tc * 512:(tc + 1) * 512]
                            dst = ac if "sh" in parts else dmy[:]
                            VS.do(lambda dst=dst, ac=ac, oc=oc: v.tensor_tensor(
                                dst, ac, pc4[oc % 4][:], OP.add), f"dash{tc}_{oc}")
                    for e in range(NE):
                        v.wait_ge(s_cb[e % 2], 16 * cb_count(e))
                        for ocI in range(ICC):
                            v.wait_ge(s_pe, MS(f"t:up{e}_{ocI}"))
                            for tc in range(4):
                                idx = 16 * e + 4 * ocI + tc
                                sl = sg[idx % 2]
                                v.wait_ge(s_sc, MS(f"s:silu{e}_{ocI}_{tc}"))
                                VS.do(lambda sl=sl, tc=tc: v.tensor_tensor(sl[:], sl[:], pc4[4 + tc][:], OP.mult))
                                VS.do(lambda sl=sl, e=e, ocI=ocI, tc=tc: v.tensor_tensor(
                                    h[:, ocI, tc * 512:(tc + 1) * 512], sl[:], cb[e % 2][:, tc, :],
                                    OP.mult), f"h{e}_{ocI}_{tc}")
                        for oc in range(CC):
                            v.wait_ge(s_pe, MS(f"t:down{e}_{oc}"))
                            for tc in range(4):
                                ac = acc[:, oc, tc * 512:(tc + 1) * 512]
                                dst = ac if "ex" in parts else dmy[:]
                                VS.do(lambda dst=dst, ac=ac, oc=oc, tc=tc: v.tensor_tensor(
                                    dst, ac, pc4[(oc % 2) * 4 + tc][:], OP.add))
                            VS.ms[f"v:dadd{e}_{oc}"] = VS.v

                @blk4.scalar
                def _(s):
                    SS.eng = s
                    for tc in range(4):
                        s.wait_ge(s_pe, MS(f"t:conv{tc}"))
                        for ocI in range(ICC):
                            idx = 4 * tc + ocI
                            if idx >= 2:
                                s.wait_ge(s_dv, MS(sg_key(idx - 2)))
                            SS.do(lambda idx=idx, ocI=ocI: s.activation(
                                sg[idx % 2][:], pc4[ocI][:], AF.Silu), f"sish{tc}_{ocI}")
                    for e in range(NE):
                        for ocI in range(ICC):
                            s.wait_ge(s_pe, MS(f"t:gate{e}_{ocI}"))
                            for tc in range(4):
                                idx = 16 + 16 * e + 4 * ocI + tc
                                s.wait_ge(s_dv, MS(sg_key(idx - 2)))
                                SS.do(lambda idx=idx, tc=tc: s.activation(
                                    sg[idx % 2][:], pc4[tc][:], AF.Silu), f"silu{e}_{ocI}_{tc}")

                @blk4.sync
                def _(sy):
                    n_do = 8 + ("comb" in tap_d) + ("ln2x" in tap_d)
                    for oc in range(CC):
                        sy.wait_ge(s_dv, MS(f"v:dadd{NE - 1}_{oc}"))
                        sy.dma_start(outv[:, oc, :], acc[:, oc, :]).then_inc(s_do, 16)
                    sy.wait_ge(s_do, 16 * n_do)

    return nc


def build(stop_after="ffn", taps=(), parts=("id", "sh", "ex")):
    ms = {}
    build_one(ms, True, stop_after, taps, parts)
    return build_one(ms, False, stop_after, taps, parts)


def _pack_par(tf, td, tmk, tmv, tmr):
    cols = []
    for a in (tf, td, tmk, tmv, tmr):
        cols.append(np.ascontiguousarray(np.asarray(a, np.float32).reshape(CC, 128).T))
    return np.ascontiguousarray(np.concatenate(cols, axis=1))


def make_in_maps(inputs):
    shared = {
        "wk": np.ascontiguousarray(np.asarray(inputs["L0_attn_W_key"], np.float32).T),
        "wv": np.ascontiguousarray(np.asarray(inputs["L0_attn_W_value"], np.float32).T),
        "wr": np.ascontiguousarray(np.asarray(inputs["L0_attn_W_receptance"], np.float32).T),
        "wo": np.ascontiguousarray(np.asarray(inputs["L0_attn_W_output"], np.float32).T),
        "par": _pack_par(inputs["L0_attn_time_first"], inputs["L0_attn_time_decay"],
                         np.asarray(inputs["L0_attn_time_mix_k"]).ravel(),
                         np.asarray(inputs["L0_attn_time_mix_v"]).ravel(),
                         np.asarray(inputs["L0_attn_time_mix_r"]).ravel()),
        "idm": np.eye(128, dtype=np.float32),
        "gw": np.ascontiguousarray(np.asarray(inputs["L0_ffn_expert_gate_w"], np.float32)),
        "uw": np.ascontiguousarray(np.asarray(inputs["L0_ffn_expert_up_w"], np.float32)),
        "dw": np.ascontiguousarray(np.asarray(inputs["L0_ffn_expert_down_w"], np.float32)),
        "cw": np.ascontiguousarray(np.asarray(inputs["L0_ffn_conv_w"], np.float32).transpose(2, 1, 0)),
        "su": np.ascontiguousarray(np.asarray(inputs["L0_ffn_shared_up_w"], np.float32)),
        "sd": np.ascontiguousarray(np.asarray(inputs["L0_ffn_shared_down_w"], np.float32)),
        "rw": np.ascontiguousarray(np.asarray(inputs["L0_ffn_router_w"], np.float32)),
        "rb": np.ascontiguousarray(np.asarray(inputs["L0_ffn_router_bias"], np.float32).reshape(16, 1)),
    }
    x = np.asarray(inputs["L0_attn_x"], np.float32)
    return [dict(shared, x=np.ascontiguousarray(x[b].T)) for b in range(B)]


def kernel(**inputs):
    nc = build()
    in_maps = make_in_maps(inputs)
    res = run_bass_kernel_spmd(nc, in_maps, core_ids=list(range(B)))
    return np.ascontiguousarray(
        np.stack([res.results[b]["out"].T for b in range(B)], axis=0)).astype(np.float32)


# version 14 (final)
# speedup vs baseline: 1.0804x; 1.0804x over previous
import numpy as np
import concourse.bass as bass
import concourse.mybir as mybir
from concourse.bass_utils import run_bass_kernel_spmd
from contextlib import ExitStack

F32 = mybir.dt.float32
BF16 = mybir.dt.bfloat16
AF = mybir.ActivationFunctionType
OP = mybir.AluOpType
AX = mybir.AxisListType

B, T, C = 8, 2048, 1024
CC = 8
L = 128
NTC = T // L
NE = 15
I = 512
ICC = 4
EPS = 1e-5


def fl(ap):
    return ap.rearrange("p a b -> p (a b)")


class ES:

    def __init__(self, eng, sem, ms, prefix, self_sync=True):
        self.eng, self.sem, self.ms, self.px = eng, sem, ms, prefix
        self.v = 0
        self.run_base = 0
        self.last_w = 0
        self.self_sync = self_sync

    def do(self, f, key=None, indep=False):
        target = self.run_base if indep else self.v
        if self.self_sync and target > 0 and target > self.last_w:
            self.eng.wait_ge(self.sem, target)
            self.last_w = target
        inst = f()
        inst.then_inc(self.sem, 1)
        self.v += 1
        if not indep:
            self.run_base = self.v
        if key is not None:
            self.ms[self.px + ":" + key] = self.v
        return self.v

    def mark(self, inst, key=None):
        inst.then_inc(self.sem, 1)
        self.v += 1
        if key is not None:
            self.ms[self.px + ":" + key] = self.v
        return self.v


def sg_key(idx):
    if idx < 16:
        return f"v:hsh{idx // 4}_{idx % 4}"
    j = idx - 16
    return f"v:h{j // 16}_{(j % 16) // 4}_{j % 4}"


def build_one(ms, rec, stop_after="ffn", taps=(), parts=("id", "sh", "ex")):
    nc = bass.Bass("TRN2", target_bir_lowering=False)

    def MS(key):
        return 0 if rec else ms[key]

    x_d = nc.dram_tensor("x", [C, T], F32, kind="ExternalInput")
    wk_d = nc.dram_tensor("wk", [C, C], F32, kind="ExternalInput")
    wv_d = nc.dram_tensor("wv", [C, C], F32, kind="ExternalInput")
    wr_d = nc.dram_tensor("wr", [C, C], F32, kind="ExternalInput")
    wo_d = nc.dram_tensor("wo", [C, C], F32, kind="ExternalInput")
    par_d = nc.dram_tensor("par", [128, 40], F32, kind="ExternalInput")
    id_d = nc.dram_tensor("idm", [128, 128], F32, kind="ExternalInput")
    gw_d = nc.dram_tensor("gw", [NE, C, I], F32, kind="ExternalInput")
    uw_d = nc.dram_tensor("uw", [NE, C, I], F32, kind="ExternalInput")
    dw_d = nc.dram_tensor("dw", [NE, I, C], F32, kind="ExternalInput")
    cw_d = nc.dram_tensor("cw", [3, C, I], F32, kind="ExternalInput")
    su_d = nc.dram_tensor("su", [C, I], F32, kind="ExternalInput")
    sd_d = nc.dram_tensor("sd", [I, C], F32, kind="ExternalInput")
    rw_d = nc.dram_tensor("rw", [C, 16], F32, kind="ExternalInput")
    rb_d = nc.dram_tensor("rb", [16, 1], F32, kind="ExternalInput")
    comb_d = nc.dram_tensor("comb_dram", [16, T], F32, kind="Internal")
    out_d = nc.dram_tensor("out", [C, T], F32, kind="ExternalOutput")
    tap_d = {}
    for tname in taps:
        shape = {"acc": [C, T], "ln2x": [C, T + 2], "comb": [16, T]}[tname]
        dt = BF16 if tname == "ln2x" else F32
        tap_d[tname] = nc.dram_tensor("tap_" + tname, shape, dt, kind="ExternalOutput")

    xv = x_d.rearrange("(cc p) t -> p cc t", p=128)
    outv = out_d.rearrange("(cc p) t -> p cc t", p=128)

    with ExitStack() as st:
        en = st.enter_context
        acc = en(nc.sbuf_tensor("acc", [128, CC, T], F32))
        par = en(nc.sbuf_tensor("par_sb", [128, 40], F32))
        ew = en(nc.sbuf_tensor("ew", [128, CC], F32))
        ones_b = en(nc.sbuf_tensor("ones_b", [128, 128], BF16))
        id_sb = en(nc.sbuf_tensor("id_sb", [128, 128], F32))

        s_pe = en(nc.semaphore("s_pe"))
        s_dv = en(nc.semaphore("s_dv"))
        s_sc = en(nc.semaphore("s_sc"))
        s_dpar = en(nc.semaphore("s_dpar"))
        s_dwa = en(nc.semaphore("s_dwa"))
        s_dwb = en(nc.semaphore("s_dwb"))
        s_dwf = en(nc.semaphore("s_dwf"))
        s_dsl = [en(nc.semaphore(f"s_dsl{i}")) for i in range(4)]
        s_cb = [en(nc.semaphore(f"s_cb{i}")) for i in range(2)]
        s_dx2 = [en(nc.semaphore(f"s_dx2{i}")) for i in range(2)]
        s_gp = en(nc.semaphore("s_gp"))
        s_dc = en(nc.semaphore("s_dc"))
        s_do = en(nc.semaphore("s_do"))

        U = lambda cc: par[:, cc:cc + 1]
        TMK = lambda cc: par[:, 16 + cc:17 + cc]
        TMV = lambda cc: par[:, 24 + cc:25 + cc]
        TMR = lambda cc: par[:, 32 + cc:33 + cc]

        VS = ES(None, s_dv, ms, "v")
        SS = ES(None, s_sc, ms, "s")
        GS = ES(None, s_gp, ms, "g")
        TS_ = ES(None, s_pe, ms, "t", self_sync=False)

        SCW = 512
        NSC = T // SCW
        with ExitStack() as s1:
            e1 = s1.enter_context
            wsl_a = [e1(nc.sbuf_tensor(f"wsla{i}", [128, CC, C], BF16)) for i in range(2)]
            ewb = e1(nc.sbuf_tensor("ewb", [128, CC, L], F32))
            xb = e1(nc.sbuf_tensor("xb", [128, CC, SCW], BF16))
            xsq = e1(nc.sbuf_tensor("xsq", [128, CC, SCW], BF16))
            ln1x = e1(nc.sbuf_tensor("ln1x", [128, CC, SCW + 1], BF16))
            mx = e1(nc.sbuf_tensor("mx", [128, CC, SCW], BF16))
            mxb = e1(nc.sbuf_tensor("mxb", [128, CC, SCW], BF16))
            k_sb = e1(nc.sbuf_tensor("k_sb", [128, CC, SCW], BF16))
            v_sb = e1(nc.sbuf_tensor("v_sb", [128, CC, SCW], BF16))
            srl = e1(nc.sbuf_tensor("srl", [128, CC, SCW], BF16))
            sry = e1(nc.sbuf_tensor("sry", [128, CC, SCW], BF16))
            m1_sb = e1(nc.sbuf_tensor("m1_sb", [128, SCW], F32))
            tA = e1(nc.sbuf_tensor("tA", [128, SCW], F32))
            e_t = e1(nc.sbuf_tensor("e_t", [128, CC, L], F32))
            a_t = e1(nc.sbuf_tensor("a_t", [128, CC, L], F32))
            av_t = e1(nc.sbuf_tensor("av_t", [128, CC, L], F32))
            den = e1(nc.sbuf_tensor("den", [128, CC, L], F32))
            w1 = e1(nc.sbuf_tensor("w1", [128, CC, L], F32))
            P_ch = e1(nc.sbuf_tensor("P_ch", [128, CC, L + 1], F32))
            Q_ch = e1(nc.sbuf_tensor("Q_ch", [128, CC, L + 1], F32))
            pb = [e1(nc.psum_tensor(f"pb{i}", [128, 512], F32)) for i in range(8)]
            blk = e1(nc.Block())

            @blk.gpsimd
            def _(g):
                GS.eng = g

                def wload(m):
                    wsrc = [wk_d, wv_d, wr_d, wo_d]
                    if m >= 4 * NSC:
                        return
                    if m >= 2:
                        g.wait_ge(s_pe, MS(f"t:MAT{m - 2}"))
                    g.dma_start(wsl_a[m % 2][:],
                                wsrc[m % 4].rearrange("(cc p) d -> p cc d", p=128)
                                ).then_inc(s_dwa if m % 2 == 0 else s_dwb, 16)

                g.dma_start(par[:], par_d[:]).then_inc(s_dpar, 16)
                g.dma_start(id_sb[:], id_d[:]).then_inc(s_dpar, 16)
                wload(0)
                wload(1)
                for s_ in range(NSC):
                    wload(4 * s_ + 2)
                    wload(4 * s_ + 3)
                    wload(4 * s_ + 4)
                    wload(4 * s_ + 5)

            @blk.sync
            def _(sy):
                for s_ in range(NSC):
                    if s_ >= 2:
                        sy.wait_ge(s_dv, MS(f"v:xb{s_ - 2}"))
                    sy.dma_start(acc[:, :, s_ * SCW:(s_ + 1) * SCW],
                                 xv[:, :, s_ * SCW:(s_ + 1) * SCW]).then_inc(s_dx2[s_ % 2], 16)

            @blk.vector
            def _(v):
                VS.eng = v
                v.wait_ge(s_sc, MS("s:ew1"))
                VS.do(lambda: v.tensor_scalar_mul(ew[:], ew[:], -1.0), "negew")
                v.wait_ge(s_sc, MS("s:ew2"))
                VS.do(lambda: v.tensor_copy(ewb[:], ew[:].unsqueeze(2).broadcast_to([128, CC, L])))
                VS.do(lambda: v.memset(ln1x[:, :, 0:1], 0.0))
                VS.do(lambda: v.memset(P_ch[:, :, 0:1], 0.0))
                VS.do(lambda: v.memset(Q_ch[:, :, 0:1], 0.0))
                VS.do(lambda: v.memset(ones_b[:], 1.0), "setup")

                def pre_cast(sp):
                    ssl = slice(sp * SCW, (sp + 1) * SCW)
                    if sp >= 1:
                        v.wait_ge(s_pe, MS(f"t:SQ1_{sp - 1}"))
                    v.wait_ge(s_dx2[sp % 2], 16 * (sp // 2 + 1))
                    VS.do(lambda ssl=ssl: v.tensor_copy(xb[:], acc[:, :, ssl]), f"xb{sp}")
                    VS.do(lambda: v.tensor_tensor(xsq[:], xb[:], xb[:], OP.mult), f"xsq{sp}")

                def pre_maux(sp):
                    v.wait_ge(s_pe, MS(f"t:SQ1_{sp}"))
                    VS.do(lambda: v.tensor_scalar_mul(m1_sb[:], pb[0][:], 1.0 / C))
                    VS.do(lambda: v.tensor_tensor(tA[:], m1_sb[:], m1_sb[:], OP.mult))
                    VS.do(lambda: v.scalar_tensor_tensor(pb[2][:], pb[1][:], 1.0 / C, tA[:], OP.mult, OP.subtract))
                    VS.do(lambda: v.tensor_scalar_add(pb[2][:], pb[2][:], EPS), f"var1_{sp}")

                def pre_normk(sp):
                    v.wait_ge(s_sc, MS(f"s:rstd1_{sp}"))
                    if sp >= 1:
                        VS.do(lambda: v.tensor_copy(ln1x[:, :, 0:1], ln1x[:, :, SCW:SCW + 1]))
                    for l in range(4):
                        lsl = slice(sp * SCW + l * L, sp * SCW + (l + 1) * L)
                        mbc = m1_sb[:, l * L:(l + 1) * L].unsqueeze(1).broadcast_to([128, CC, L])
                        rbc = pb[2][:, l * L:(l + 1) * L].unsqueeze(1).broadcast_to([128, CC, L])
                        VS.do(lambda lsl=lsl, mbc=mbc: v.tensor_tensor(w1[:], acc[:, :, lsl], mbc, OP.subtract))
                        VS.do(lambda l=l, rbc=rbc: v.tensor_tensor(
                            ln1x[:, :, 1 + l * L:1 + (l + 1) * L], w1[:], rbc, OP.mult))
                    VS.ms[f"v:ln1_{sp}"] = VS.v
                    VS.do(lambda: v.tensor_tensor(xsq[:], ln1x[:, :, 1:], ln1x[:, :, 0:SCW], OP.subtract))
                    if sp >= 1:
                        v.wait_ge(s_pe, MS(f"t:MAT{4 * (sp - 1) + 2}"))
                    for cc in range(CC):
                        VS.do(lambda cc=cc: v.scalar_tensor_tensor(
                            mx[:, cc, :], xsq[:, cc, :], TMK(cc), ln1x[:, cc, 0:SCW], OP.mult, OP.add),
                            indep=True)
                    VS.ms[f"v:mixk{sp}"] = VS.v

                def pre_mixv(sp):
                    for cc in range(CC):
                        VS.do(lambda cc=cc: v.scalar_tensor_tensor(
                            mxb[:, cc, :], xsq[:, cc, :], TMV(cc), ln1x[:, cc, 0:SCW], OP.mult, OP.add),
                            indep=True)
                    VS.ms[f"v:mixv{sp}"] = VS.v

                pre_cast(0)
                pre_maux(0)
                pre_normk(0)
                pre_mixv(0)
                for s_ in range(NSC):
                    ssl = slice(s_ * SCW, (s_ + 1) * SCW)
                    v.wait_ge(s_pe, MS(f"t:MAT{4 * s_ + 0}"))
                    for cc in range(CC):
                        VS.do(lambda cc=cc: v.scalar_tensor_tensor(
                            mx[:, cc, :], xsq[:, cc, :], TMR(cc), ln1x[:, cc, 0:SCW], OP.mult, OP.add),
                            indep=True)
                    VS.ms[f"v:mixr{s_}"] = VS.v
                    if s_ >= 1:
                        psl = slice((s_ - 1) * SCW, s_ * SCW)
                        v.wait_ge(s_pe, MS(f"t:MAT{4 * (s_ - 1) + 3}"))
                        for od in range(CC):
                            VS.do(lambda od=od, psl=psl: v.tensor_tensor(
                                acc[:, od, psl], acc[:, od, psl], pb[od][:], OP.add), indep=True)
                        VS.ms[f"v:accev{s_ - 1}"] = VS.v
                    if s_ + 1 < NSC:
                        pre_cast(s_ + 1)
                    for l in range(4):
                        ksl = slice(l * L, (l + 1) * L)
                        v.wait_ge(s_sc, MS(f"s:ea{s_}_{l}"))
                        if l == 0:
                            v.wait_ge(s_sc, MS(f"s:vevac{s_}"))
                        VS.do(lambda ksl=ksl: v.tensor_tensor(w1[:], e_t[:], v_sb[:, :, ksl], OP.mult))
                        VS.do(lambda ksl=ksl: v.tensor_tensor(av_t[:], a_t[:], v_sb[:, :, ksl], OP.mult))
                        if not (s_ == 0 and l == 0):
                            VS.do(lambda: v.tensor_copy(P_ch[:, :, 0:1], P_ch[:, :, L:L + 1]))
                            VS.do(lambda: v.tensor_copy(Q_ch[:, :, 0:1], Q_ch[:, :, L:L + 1]))
                        for cc in range(CC):
                            VS.do(lambda cc=cc: v.tensor_tensor_scan(
                                P_ch[:, cc, 1:], ewb[:, cc, :], w1[:, cc, :],
                                P_ch[:, cc, 0:1], OP.mult, OP.add), indep=True)
                        for cc in range(CC):
                            VS.do(lambda cc=cc: v.tensor_tensor_scan(
                                Q_ch[:, cc, 1:], ewb[:, cc, :], e_t[:, cc, :],
                                Q_ch[:, cc, 0:1], OP.mult, OP.add), indep=True)
                        VS.do(lambda: v.tensor_tensor(den[:], Q_ch[:, :, 0:L], a_t[:], OP.add), f"den{s_}_{l}")
                        VS.do(lambda: v.tensor_tensor(w1[:], P_ch[:, :, 0:L], av_t[:], OP.add))
                        v.wait_ge(s_sc, MS(f"s:rden{s_}_{l}"))
                        VS.do(lambda: v.tensor_tensor(w1[:], w1[:], e_t[:], OP.mult))
                        VS.do(lambda ksl=ksl: v.tensor_tensor(
                            sry[:, :, ksl], srl[:, :, ksl], w1[:], OP.mult), f"sry{s_}_{l}")
                        if s_ + 1 < NSC:
                            if l == 0:
                                pre_maux(s_ + 1)
                            elif l == 1:
                                pre_normk(s_ + 1)
                            elif l == 2:
                                pre_mixv(s_ + 1)
                sN = NSC - 1
                v.wait_ge(s_pe, MS(f"t:MAT{4 * sN + 3}"))
                for od in range(CC):
                    VS.do(lambda od=od: v.tensor_tensor(
                        acc[:, od, sN * SCW:(sN + 1) * SCW],
                        acc[:, od, sN * SCW:(sN + 1) * SCW], pb[od][:], OP.add), indep=True)
                VS.ms[f"v:accev{sN}"] = VS.v

            @blk.scalar
            def _(s):
                SS.eng = s
                s.wait_ge(s_dpar, 32)
                SS.do(lambda: s.activation(ew[:], par[:, 8:16], AF.Exp), "ew1")
                s.wait_ge(s_dv, MS("v:negew"))
                SS.do(lambda: s.activation(ew[:], ew[:], AF.Exp), "ew2")

                def emit_rstd(sp):
                    s.wait_ge(s_dv, MS(f"v:var1_{sp}"))
                    SS.do(lambda: s.activation(pb[2][:], pb[2][:], AF.Ln))
                    SS.do(lambda: s.activation(pb[2][:], pb[2][:], AF.Exp, scale=-0.5), f"rstd1_{sp}")

                def emit_ea(s_, l):
                    ksl = slice(l * L, (l + 1) * L)
                    if l >= 1:
                        s.wait_ge(s_dv, MS(f"v:sry{s_}_{l - 1}"))
                    elif s_ >= 1:
                        s.wait_ge(s_dv, MS(f"v:sry{s_ - 1}_3"))
                    SS.do(lambda ksl=ksl: s.activation(e_t[:], k_sb[:, :, ksl], AF.Exp))
                    for cc in range(CC):
                        SS.do(lambda cc=cc, ksl=ksl: s.activation(
                            a_t[:, cc, :], k_sb[:, cc, ksl], AF.Exp, bias=U(cc)), indep=True)
                    SS.ms[f"s:ea{s_}_{l}"] = SS.v

                def emit_rden(s_, l):
                    s.wait_ge(s_dv, MS(f"v:den{s_}_{l}"))
                    SS.do(lambda: s.activation(fl(den[:]), fl(den[:]), AF.Ln))
                    SS.do(lambda: s.activation(fl(e_t[:]), fl(den[:]), AF.Exp, scale=-1.0),
                          f"rden{s_}_{l}")

                emit_rstd(0)
                for s_ in range(NSC):
                    s.wait_ge(s_pe, MS(f"t:MAT{4 * s_ + 0}"))
                    for od in range(CC):
                        SS.do(lambda od=od: s.copy(k_sb[:, od, :], pb[od][:]), indep=True)
                    SS.ms[f"s:kevac{s_}"] = SS.v
                    emit_ea(s_, 0)
                    s.wait_ge(s_pe, MS(f"t:MAT{4 * s_ + 1}"))
                    for od in range(CC):
                        SS.do(lambda od=od: s.copy(v_sb[:, od, :], pb[od][:]), indep=True)
                    SS.ms[f"s:vevac{s_}"] = SS.v
                    s.wait_ge(s_pe, MS(f"t:MAT{4 * s_ + 2}"))
                    for od in range(CC):
                        SS.do(lambda od=od: s.activation(pb[od][:], pb[od][:], AF.Exp, scale=-1.0))
                        SS.do(lambda od=od: s.activation(pb[od][:], pb[od][:], AF.Ln, bias=1.0))
                        SS.do(lambda od=od: s.activation(srl[:, od, :], pb[od][:], AF.Exp, scale=-1.0),
                              f"sigread_{s_}" if od == CC - 1 else None)
                    SS.ms[f"s:sig{s_}"] = SS.v
                    emit_rden(s_, 0)
                    emit_ea(s_, 1)
                    emit_rden(s_, 1)
                    if s_ + 1 < NSC:
                        emit_rstd(s_ + 1)
                    for l in range(2, 4):
                        emit_ea(s_, l)
                        emit_rden(s_, l)

            @blk.tensor
            def _(t):
                TS_.eng = t
                t.wait_ge(s_dv, MS("v:setup"))

                def emit_stats(sp):
                    t.wait_ge(s_dv, MS(f"v:xsq{sp}"))
                    if sp >= 1:
                        t.wait_ge(s_sc, MS(f"s:sigread_{sp - 1}"))
                    for cc in range(CC):
                        inst = t.matmul(pb[0][:], ones_b[:], xb[:, cc, :], start=(cc == 0), stop=(cc == CC - 1))
                    TS_.mark(inst, f"S1_{sp}")
                    for cc in range(CC):
                        inst = t.matmul(pb[1][:], ones_b[:], xsq[:, cc, :], start=(cc == 0), stop=(cc == CC - 1))
                    TS_.mark(inst, f"SQ1_{sp}")

                def emit_mat(m, rhs_t):
                    for od in range(CC):
                        for kc in range(CC):
                            inst = t.matmul(pb[od][:],
                                            wsl_a[m % 2][:, kc, od * 128:(od + 1) * 128],
                                            rhs_t[:, kc, :],
                                            start=(kc == 0), stop=(kc == CC - 1))
                    TS_.mark(inst, f"MAT{m}")

                emit_stats(0)
                t.wait_ge(s_dwa, 16)
                t.wait_ge(s_dv, MS("v:mixk0"))
                emit_mat(0, mx)
                for s_ in range(NSC):
                    m = 4 * s_ + 1
                    t.wait_ge(s_dwa if m % 2 == 0 else s_dwb, 16 * (m // 2 + 1))
                    t.wait_ge(s_dv, MS(f"v:mixv{s_}"))
                    t.wait_ge(s_sc, MS(f"s:kevac{s_}"))
                    if s_ >= 1:
                        t.wait_ge(s_dv, MS(f"v:accev{s_ - 1}"))
                    emit_mat(m, mxb)
                    m = 4 * s_ + 2
                    t.wait_ge(s_dwa if m % 2 == 0 else s_dwb, 16 * (m // 2 + 1))
                    t.wait_ge(s_dv, MS(f"v:mixr{s_}"))
                    t.wait_ge(s_sc, MS(f"s:vevac{s_}"))
                    emit_mat(m, mx)
                    if s_ + 1 < NSC:
                        emit_stats(s_ + 1)
                        m = 4 * (s_ + 1)
                        t.wait_ge(s_dwa if m % 2 == 0 else s_dwb, 16 * (m // 2 + 1))
                        t.wait_ge(s_dv, MS(f"v:mixk{s_ + 1}"))
                        t.wait_ge(s_sc, MS(f"s:sigread_{s_}"))
                        emit_mat(m, mx)
                    m = 4 * s_ + 3
                    t.wait_ge(s_dwa if m % 2 == 0 else s_dwb, 16 * (m // 2 + 1))
                    t.wait_ge(s_dv, MS(f"v:sry{s_}_3"))
                    if s_ + 1 < NSC:
                        t.wait_ge(s_sc, MS(f"s:kevac{s_ + 1}"))
                    emit_mat(m, sry)

        if stop_after == "attn":
            with nc.Block() as blk9:
                @blk9.sync
                def _(sy):
                    n = 1
                    sy.dma_start(outv, acc[:]).then_inc(s_do, 16)
                    if "acc" in tap_d:
                        sy.dma_start(tap_d["acc"].rearrange("(cc p) t -> p cc t", p=128),
                                     acc[:]).then_inc(s_do, 16)
                        n += 1
                    sy.wait_ge(s_do, 16 * n)
            return nc

        with ExitStack() as s2:
            e2 = s2.enter_context
            ln2x = e2(nc.sbuf_tensor("ln2x", [128, CC, T + 2], BF16))
            cwS = e2(nc.sbuf_tensor("cw_sb", [128, 24, I], BF16))
            suS = e2(nc.sbuf_tensor("su_sb", [128, CC, I], BF16))
            sdS = e2(nc.sbuf_tensor("sd_sb", [128, ICC, C], BF16))
            rwS = e2(nc.sbuf_tensor("rw_sb", [128, CC, 16], BF16))
            rbS = e2(nc.sbuf_tensor("rb_sb", [16, 1], F32))
            wsl = [e2(nc.sbuf_tensor(f"wsl{i}", [128, CC, I], BF16)) for i in range(4)]

            def slot_ap(m):
                s = wsl[m % 4]
                if m % 3 == 2:
                    return s[:].rearrange("p a b -> p (a b)").rearrange("p (a b) -> p a b", b=C)
                return s[:]

            with ExitStack() as sb2:
                eb2 = sb2.enter_context
                accb = eb2(nc.sbuf_tensor("accb", [128, CC, 512], BF16))
                xsqb = eb2(nc.sbuf_tensor("xsqb", [128, CC, 512], BF16))
                m_sb = eb2(nc.sbuf_tensor("m_sb", [128, 512], F32))
                vr_sb = eb2(nc.sbuf_tensor("vr_sb", [128, 512], F32))
                sgl = eb2(nc.sbuf_tensor("sgl", [128, 512], F32))
                pc = [eb2(nc.psum_tensor(f"pc{i}", [128, 512], F32)) for i in range(2)]
                blk2 = eb2(nc.Block())

                @blk2.gpsimd
                def _(g):
                    g.dma_start(cwS[:], cw_d.rearrange("k (cc p) o -> p (k cc) o", p=128)).then_inc(s_dwf, 16)
                    g.dma_start(suS[:], su_d.rearrange("(cc p) o -> p cc o", p=128)).then_inc(s_dwf, 16)
                    g.dma_start(sdS[:], sd_d.rearrange("(cc p) o -> p cc o", p=128)).then_inc(s_dwf, 16)
                    g.dma_start(rwS[:], rw_d.rearrange("(cc p) o -> p cc o", p=128)).then_inc(s_dwf, 16)
                    g.dma_start(rbS[:], rb_d[:]).then_inc(s_dwf, 16)
                    for m in range(4):
                        e, j = m // 3, m % 3
                        src = (gw_d, uw_d, dw_d)[j]
                        view = src[e].rearrange("(cc p) o -> p cc o", p=128)
                        g.dma_start(slot_ap(m), view).then_inc(s_dsl[m % 4], 16)

                @blk2.vector
                def _(v):
                    VS.eng = v
                    VS.do(lambda: v.tensor_copy(accb[:], acc[:, :, 0:512]), "cast0")
                    VS.do(lambda: v.tensor_tensor(xsqb[:], accb[:], accb[:], OP.mult), "cast2_0")
                    for g_ in range(4):
                        gs = slice(g_ * 512, (g_ + 1) * 512)
                        v.wait_ge(s_pe, MS(f"t:SQ2_{g_}"))
                        VS.do(lambda: v.tensor_scalar_mul(m_sb[:], pc[0][:], 1.0 / C))
                        VS.do(lambda: v.tensor_tensor(vr_sb[:], m_sb[:], m_sb[:], OP.mult))
                        VS.do(lambda: v.scalar_tensor_tensor(vr_sb[:], pc[1][:], 1.0 / C, vr_sb[:],
                                                             OP.mult, OP.subtract))
                        VS.do(lambda: v.tensor_scalar_add(vr_sb[:], vr_sb[:], EPS), f"var2_{g_}")
                        if g_ + 1 < 4:
                            ns = slice((g_ + 1) * 512, (g_ + 2) * 512)
                            VS.do(lambda ns=ns: v.tensor_copy(accb[:], acc[:, :, ns]), f"cast{g_ + 1}")
                            VS.do(lambda: v.tensor_tensor(xsqb[:], accb[:], accb[:], OP.mult), f"cast2_{g_ + 1}")
                        v.wait_ge(s_sc, MS(f"s:rstd2_{g_}"))
                        for cc in range(CC):
                            VS.do(lambda cc=cc, gs=gs: v.tensor_tensor(sgl[:], acc[:, cc, gs], m_sb[:], OP.subtract))
                            VS.do(lambda cc=cc, g_=g_: v.tensor_tensor(
                                ln2x[:, cc, 2 + g_ * 512: 2 + (g_ + 1) * 512], sgl[:], vr_sb[:], OP.mult))
                        VS.ms[f"v:norm{g_}"] = VS.v
                    VS.do(lambda: v.memset(ln2x[:, :, 0:2], 0.0), "ln2pad")

                @blk2.scalar
                def _(s):
                    SS.eng = s
                    for g_ in range(4):
                        s.wait_ge(s_dv, MS(f"v:var2_{g_}"))
                        SS.do(lambda: s.activation(vr_sb[:], vr_sb[:], AF.Ln))
                        SS.do(lambda: s.activation(vr_sb[:], vr_sb[:], AF.Exp, scale=-0.5), f"rstd2_{g_}")

                @blk2.tensor
                def _(t):
                    for g_ in range(4):
                        t.wait_ge(s_dv, MS(f"v:cast2_{g_}"))
                        for cc in range(CC):
                            inst = t.matmul(pc[0][:], ones_b[:], accb[:, cc, :],
                                            start=(cc == 0), stop=(cc == CC - 1))
                        TS_.mark(inst, f"S2_{g_}")
                        for cc in range(CC):
                            inst = t.matmul(pc[1][:], ones_b[:], xsqb[:, cc, :],
                                            start=(cc == 0), stop=(cc == CC - 1))
                        TS_.mark(inst, f"SQ2_{g_}")

            if stop_after == "ln2":
                with nc.Block() as blk9:
                    @blk9.sync
                    def _(sy):
                        n = 1
                        sy.dma_start(outv, acc[:]).then_inc(s_do, 16)
                        if "ln2x" in tap_d:
                            sy.dma_start(tap_d["ln2x"].rearrange("(cc p) t -> p cc t", p=128),
                                         ln2x[:]).then_inc(s_do, 16)
                            n += 1
                        sy.wait_ge(s_do, 16 * n)
                return nc

            with ExitStack() as sb3:
                eb3 = sb3.enter_context
                lg = eb3(nc.sbuf_tensor("lg", [16, T], F32))
                ptm = eb3(nc.sbuf_tensor("ptm", [128, 16, 16], F32))
                ptm2 = eb3(nc.sbuf_tensor("ptm2", [128, 16, 16], F32))
                psv = eb3(nc.sbuf_tensor("psv", [128, 16, 16], F32))
                flsh = eb3(nc.sbuf_tensor("flsh", [128, 16], F32))
                ctm = eb3(nc.sbuf_tensor("ctm", [128, 16, 16], F32))
                cfm = eb3(nc.sbuf_tensor("cfm", [16, T], F32))
                aux = eb3(nc.sbuf_tensor("aux", [128, 16, 4], F32))
                pc3 = [eb3(nc.psum_tensor(f"pc3{i}", [128, 512], F32)) for i in range(8)]
                blk3 = eb3(nc.Block())

                @blk3.tensor
                def _(t):
                    t.wait_ge(s_dwf, 16 * 5)
                    for tc in range(4):
                        for kc in range(CC):
                            inst = t.matmul(pc3[4 + tc][0:16, :], rwS[:, kc, :],
                                            ln2x[:, kc, 2 + tc * 512: 2 + (tc + 1) * 512],
                                            start=(kc == 0), stop=(kc == CC - 1))
                        TS_.mark(inst, f"lg{tc}")
                    t.wait_ge(s_dv, MS("v:lgev"))
                    for i in range(16):
                        inst = t.matmul(pc3[0][:, i * 16:(i + 1) * 16],
                                        lg[0:16, i * 128:(i + 1) * 128],
                                        id_sb[0:16, 0:16], start=True, stop=True)
                        TS_.mark(inst, f"fwdT{i}" if i < 15 else "fwdT")
                    t.wait_ge(s_dv, MS("v:smax"))
                    for i in range(16):
                        inst = t.matmul(pc3[4 + i // 4][0:16, (i % 4) * 128:(i % 4 + 1) * 128],
                                        fl(ctm)[:, i * 16:(i + 1) * 16],
                                        id_sb[:, 0:128], start=True, stop=True)
                        TS_.mark(inst, f"backT{i}" if i < 15 else "backT")

                @blk3.vector
                def _(v):
                    VS.eng = v
                    v.wait_ge(s_pe, MS("t:lg3"))
                    for tc in range(4):
                        VS.do(lambda tc=tc: v.tensor_scalar_add(
                            lg[0:16, tc * 512:(tc + 1) * 512], pc3[4 + tc][0:16, :], rbS[:]))
                    VS.ms["v:lgev"] = VS.v
                    v.wait_ge(s_pe, MS("t:fwdT"))
                    VS.do(lambda: v.tensor_copy(fl(ptm), pc3[0][:, 0:256]))
                    VS.do(lambda: v.reduce_max(aux[:, :, 0:1], ptm[:], axis=AX.X))
                    VS.do(lambda: v.tensor_tensor(psv[:], ptm[:],
                                                  aux[:, :, 0:1].broadcast_to([128, 16, 16]), OP.subtract),
                          "presoft")
                    v.wait_ge(s_sc, MS("s:sexp"))
                    VS.do(lambda: v.reduce_sum(aux[:, :, 1:2], ptm2[:], axis=AX.X))
                    VS.do(lambda: v.reciprocal(fl(aux[:, :, 2:3]), fl(aux[:, :, 1:2])))
                    VS.do(lambda: v.tensor_tensor(ptm[:], ptm2[:], aux[:, :, 2:3].broadcast_to([128, 16, 16]), OP.mult))
                    VS.do(lambda: v.reduce_max(aux[:, :, 0:1], ptm[:], axis=AX.X))
                    VS.do(lambda: v.tensor_tensor(ctm[:], ptm[:], aux[:, :, 0:1].broadcast_to([128, 16, 16]), OP.is_equal))
                    VS.do(lambda: v.scalar_tensor_tensor(psv[:], ctm[:], -1e30, ptm[:], OP.mult, OP.add))
                    VS.do(lambda: v.reduce_max(aux[:, :, 1:2], psv[:], axis=AX.X))
                    VS.do(lambda: v.tensor_tensor(ctm[:], ptm[:], aux[:, :, 1:2].broadcast_to([128, 16, 16]), OP.is_ge))
                    VS.do(lambda: v.tensor_tensor(aux[:, :, 3:4], aux[:, :, 0:1], aux[:, :, 1:2], OP.add))
                    VS.do(lambda: v.tensor_scalar_add(aux[:, :, 0:1], aux[:, :, 3:4], 1e-6))
                    VS.do(lambda: v.reciprocal(fl(aux[:, :, 3:4]), fl(aux[:, :, 0:1])))
                    VS.do(lambda: v.tensor_tensor(psv[:], ctm[:], ptm[:], OP.mult))
                    VS.do(lambda: v.tensor_tensor(ctm[:], psv[:], aux[:, :, 3:4].broadcast_to([128, 16, 16]), OP.mult),
                          "smax")
                    v.wait_ge(s_pe, MS("t:backT"))
                    for i in range(4):
                        VS.do(lambda i=i: v.tensor_copy(cfm[0:16, i * 512:(i + 1) * 512], pc3[4 + i][0:16, :]))
                    VS.ms["v:cfmev"] = VS.v

                @blk3.scalar
                def _(s):
                    SS.eng = s
                    s.wait_ge(s_dv, MS("v:presoft"))
                    SS.do(lambda: s.activation(fl(ptm2), fl(psv), AF.Exp), "sexp")

                @blk3.sync
                def _(sy):
                    sy.wait_ge(s_dv, MS("v:cfmev"))
                    sy.dma_start(comb_d[:], cfm[:]).then_inc(s_dc, 16)
                    sy.wait_ge(s_dc, 16)
                    if "comb" in tap_d:
                        sy.dma_start(tap_d["comb"][:], cfm[:]).then_inc(s_do, 16)
                    if "ln2x" in tap_d:
                        sy.dma_start(tap_d["ln2x"].rearrange("(cc p) t -> p cc t", p=128),
                                     ln2x[:]).then_inc(s_do, 16)

            if stop_after == "router":
                with nc.Block() as blk9:
                    @blk9.sync
                    def _(sy):
                        sy.dma_start(outv, acc[:]).then_inc(s_do, 16)
                        sy.wait_ge(s_do, 16 * (1 + ("comb" in tap_d) + ("ln2x" in tap_d)))
                return nc

            with ExitStack() as sb4:
                eb4 = sb4.enter_context
                h = eb4(nc.sbuf_tensor("h", [128, ICC, T], BF16))
                cb = [eb4(nc.sbuf_tensor(f"cb{i}", [128, 4, 512], BF16)) for i in range(2)]
                sg = [eb4(nc.sbuf_tensor(f"sg{i}", [128, 512], F32)) for i in range(2)]
                dmy = eb4(nc.sbuf_tensor("dmy", [128, 512], F32))
                pc4 = [eb4(nc.psum_tensor(f"pc4{i}", [128, 512], F32)) for i in range(8)]
                blk4 = eb4(nc.Block())

                gp_seq = ["comb15", "comb0", "comb1"]
                for m in range(4, 3 * NE):
                    e_of = m // 3
                    if m % 3 == 0 and e_of >= 2:
                        gp_seq.append(f"comb{e_of}")
                    gp_seq.append(f"m{m}")

                def cb_count(e):
                    if e == 15:
                        return 1
                    return (e // 2 + 1) if e % 2 == 0 else ((e + 1) // 2 + 1)

                def consumed_key(m):
                    e, j = m // 3, m % 3
                    return (f"t:gate{e}_3", f"t:up{e}_3", f"t:down{e}_7")[j]

                @blk4.gpsimd
                def _(g):
                    for nm in gp_seq:
                        if nm.startswith("comb"):
                            e = 15 if nm == "comb15" else int(nm[4:])
                            buf = cb[1] if nm == "comb15" else cb[e % 2]
                            if nm == "comb1":
                                g.wait_ge(s_dv, MS("v:id3"))
                            elif nm != "comb15" and nm != "comb0" and e >= 2:
                                g.wait_ge(s_dv, MS(f"v:h{e - 2}_3_3"))
                            g.dma_start(fl(buf[:]),
                                        comb_d[e][None, :].broadcast_to([128, T])
                                        ).then_inc(s_cb[1 if nm == "comb15" else e % 2], 16)
                        else:
                            m = int(nm[1:])
                            e, j = m // 3, m % 3
                            g.wait_ge(s_pe, MS(consumed_key(m - 4)))
                            src = (gw_d, uw_d, dw_d)[j]
                            view = src[e].rearrange("(cc p) o -> p cc o", p=128)
                            g.dma_start(slot_ap(m), view).then_inc(s_dsl[m % 4], 16)

                @blk4.tensor
                def _(t):
                    t.wait_ge(s_dwf, 16 * 5)
                    for tc in range(4):
                        t0 = 2 + tc * 512
                        if tc >= 1:
                            t.wait_ge(s_dv, MS(f"v:dash{tc - 1}_7"))
                        for ocI in range(ICC):
                            for kk in range(3):
                                for kc in range(CC):
                                    inst = t.matmul(pc4[ocI][:],
                                                    cwS[:, kk * 8 + kc, ocI * 128:(ocI + 1) * 128],
                                                    ln2x[:, kc, t0 - 2 + kk: t0 - 2 + kk + 512],
                                                    start=(kk == 0 and kc == 0),
                                                    stop=(kk == 2 and kc == CC - 1))
                        TS_.mark(inst, f"conv{tc}")
                        for ocI in range(ICC):
                            for kc in range(CC):
                                inst = t.matmul(pc4[4 + ocI][:],
                                                suS[:, kc, ocI * 128:(ocI + 1) * 128],
                                                ln2x[:, kc, t0:t0 + 512],
                                                start=(kc == 0), stop=(kc == CC - 1))
                        TS_.mark(inst, f"upsh{tc}")
                        for oc in range(CC):
                            t.wait_ge(s_sc, MS(f"s:sish{tc}_3"))
                            t.wait_ge(s_dv, MS(f"v:hsh{tc}_3"))
                            if oc >= 4:
                                t.wait_ge(s_dv, MS(f"v:dash{tc}_{oc - 4}"))
                            for kcI in range(ICC):
                                inst = t.matmul(pc4[oc % 4][:],
                                                sdS[:, kcI, oc * 128:(oc + 1) * 128],
                                                h[:, kcI, tc * 512:(tc + 1) * 512],
                                                start=(kcI == 0), stop=(kcI == ICC - 1))
                            TS_.mark(inst, f"dnsh{tc}_{oc}")
                    for e in range(NE):
                        for ocI in range(ICC):
                            t.wait_ge(s_dsl[(3 * e) % 4], 16 * ((3 * e) // 4 + 1))
                            if e == 0 and ocI == 0:
                                t.wait_ge(s_dv, MS("v:dash3_7"))
                            elif ocI == 0:
                                t.wait_ge(s_dv, MS(f"v:dadd{e - 1}_6"))
                            else:
                                t.wait_ge(s_sc, MS(f"s:silu{e}_{ocI - 1}_3"))
                            for kc in range(CC):
                                for tc in range(4):
                                    inst = t.matmul(pc4[tc][:],
                                                    slot_ap(3 * e)[:, kc, ocI * 128:(ocI + 1) * 128],
                                                    ln2x[:, kc, 2 + tc * 512: 2 + (tc + 1) * 512],
                                                    start=(kc == 0), stop=(kc == CC - 1))
                            TS_.mark(inst, f"gate{e}_{ocI}")
                            t.wait_ge(s_dsl[(3 * e + 1) % 4], 16 * ((3 * e + 1) // 4 + 1))
                            if e == 0 and ocI == 0:
                                pass
                            elif ocI == 0:
                                t.wait_ge(s_dv, MS(f"v:dadd{e - 1}_7"))
                            else:
                                t.wait_ge(s_dv, MS(f"v:h{e}_{ocI - 1}_3"))
                            for kc in range(CC):
                                for tc in range(4):
                                    inst = t.matmul(pc4[4 + tc][:],
                                                    slot_ap(3 * e + 1)[:, kc, ocI * 128:(ocI + 1) * 128],
                                                    ln2x[:, kc, 2 + tc * 512: 2 + (tc + 1) * 512],
                                                    start=(kc == 0), stop=(kc == CC - 1))
                            TS_.mark(inst, f"up{e}_{ocI}")
                        t.wait_ge(s_dsl[(3 * e + 2) % 4], 16 * ((3 * e + 2) // 4 + 1))
                        t.wait_ge(s_dv, MS(f"v:h{e}_3_3"))
                        for oc in range(CC):
                            if oc >= 2:
                                t.wait_ge(s_dv, MS(f"v:dadd{e}_{oc - 2}"))
                            for kcI in range(ICC):
                                for tc in range(4):
                                    inst = t.matmul(pc4[(oc % 2) * 4 + tc][:],
                                                    slot_ap(3 * e + 2)[:, kcI, oc * 128:(oc + 1) * 128],
                                                    h[:, kcI, tc * 512:(tc + 1) * 512],
                                                    start=(kcI == 0), stop=(kcI == ICC - 1))
                            TS_.mark(inst, f"down{e}_{oc}")

                @blk4.vector
                def _(v):
                    VS.eng = v
                    v.wait_ge(s_cb[1], 16)
                    for tc in range(4):
                        cbc = cb[1][:, tc, :].unsqueeze(1).broadcast_to([128, CC, 512])
                        hf = fl(h[:]).bitcast(F32).rearrange("p (cc t) -> p cc t", t=512)
                        VS.do(lambda tc=tc, cbc=cbc, hf=hf: v.tensor_tensor(
                            hf[:], ln2x[:, :, 2 + tc * 512: 2 + (tc + 1) * 512], cbc, OP.mult))
                        dst = acc[:, :, tc * 512:(tc + 1) * 512] if "id" in parts else hf[:]
                        VS.do(lambda tc=tc, dst=dst, hf=hf: v.tensor_tensor(
                            dst, acc[:, :, tc * 512:(tc + 1) * 512], hf[:], OP.add), f"id{tc}")
                    for tc in range(4):
                        for ocI in range(ICC):
                            idx = 4 * tc + ocI
                            sl = sg[idx % 2]
                            v.wait_ge(s_sc, MS(f"s:sish{tc}_{ocI}"))
                            if ocI == 0:
                                v.wait_ge(s_pe, MS(f"t:upsh{tc}"))
                            VS.do(lambda sl=sl, ocI=ocI, tc=tc: v.tensor_tensor(
                                h[:, ocI, tc * 512:(tc + 1) * 512], sl[:], pc4[4 + ocI][:],
                                OP.mult), f"hsh{tc}_{ocI}")
                        for oc in range(CC):
                            v.wait_ge(s_pe, MS(f"t:dnsh{tc}_{oc}"))
                            ac = acc[:, oc, tc * 512:(tc + 1) * 512]
                            dst = ac if "sh" in parts else dmy[:]
                            VS.do(lambda dst=dst, ac=ac, oc=oc: v.tensor_tensor(
                                dst, ac, pc4[oc % 4][:], OP.add), f"dash{tc}_{oc}")
                    for e in range(NE):
                        v.wait_ge(s_cb[e % 2], 16 * cb_count(e))
                        for ocI in range(ICC):
                            v.wait_ge(s_pe, MS(f"t:up{e}_{ocI}"))
                            for tc in range(4):
                                idx = 16 * e + 4 * ocI + tc
                                sl = sg[idx % 2]
                                v.wait_ge(s_sc, MS(f"s:silu{e}_{ocI}_{tc}"))
                                VS.do(lambda sl=sl, tc=tc: v.tensor_tensor(sl[:], sl[:], pc4[4 + tc][:], OP.mult))
                                VS.do(lambda sl=sl, e=e, ocI=ocI, tc=tc: v.tensor_tensor(
                                    h[:, ocI, tc * 512:(tc + 1) * 512], sl[:], cb[e % 2][:, tc, :],
                                    OP.mult), f"h{e}_{ocI}_{tc}")
                        for oc in range(CC):
                            v.wait_ge(s_pe, MS(f"t:down{e}_{oc}"))
                            for tc in range(4):
                                ac = acc[:, oc, tc * 512:(tc + 1) * 512]
                                dst = ac if "ex" in parts else dmy[:]
                                VS.do(lambda dst=dst, ac=ac, oc=oc, tc=tc: v.tensor_tensor(
                                    dst, ac, pc4[(oc % 2) * 4 + tc][:], OP.add), indep=True)
                            VS.ms[f"v:dadd{e}_{oc}"] = VS.v

                @blk4.scalar
                def _(s):
                    SS.eng = s
                    for tc in range(4):
                        s.wait_ge(s_pe, MS(f"t:conv{tc}"))
                        for ocI in range(ICC):
                            idx = 4 * tc + ocI
                            if idx >= 2:
                                s.wait_ge(s_dv, MS(sg_key(idx - 2)))
                            SS.do(lambda idx=idx, ocI=ocI: s.activation(
                                sg[idx % 2][:], pc4[ocI][:], AF.Silu), f"sish{tc}_{ocI}")
                    for e in range(NE):
                        for ocI in range(ICC):
                            s.wait_ge(s_pe, MS(f"t:gate{e}_{ocI}"))
                            for tc in range(4):
                                idx = 16 + 16 * e + 4 * ocI + tc
                                s.wait_ge(s_dv, MS(sg_key(idx - 2)))
                                SS.do(lambda idx=idx, tc=tc: s.activation(
                                    sg[idx % 2][:], pc4[tc][:], AF.Silu), f"silu{e}_{ocI}_{tc}")

                @blk4.sync
                def _(sy):
                    n_do = 8 + ("comb" in tap_d) + ("ln2x" in tap_d)
                    for oc in range(CC):
                        sy.wait_ge(s_dv, MS(f"v:dadd{NE - 1}_{oc}"))
                        sy.dma_start(outv[:, oc, :], acc[:, oc, :]).then_inc(s_do, 16)
                    sy.wait_ge(s_do, 16 * n_do)

    return nc


def build(stop_after="ffn", taps=(), parts=("id", "sh", "ex")):
    ms = {}
    build_one(ms, True, stop_after, taps, parts)
    return build_one(ms, False, stop_after, taps, parts)


def _pack_par(tf, td, tmk, tmv, tmr):
    cols = []
    for a in (tf, td, tmk, tmv, tmr):
        cols.append(np.ascontiguousarray(np.asarray(a, np.float32).reshape(CC, 128).T))
    return np.ascontiguousarray(np.concatenate(cols, axis=1))


def make_in_maps(inputs):
    shared = {
        "wk": np.ascontiguousarray(np.asarray(inputs["L0_attn_W_key"], np.float32).T),
        "wv": np.ascontiguousarray(np.asarray(inputs["L0_attn_W_value"], np.float32).T),
        "wr": np.ascontiguousarray(np.asarray(inputs["L0_attn_W_receptance"], np.float32).T),
        "wo": np.ascontiguousarray(np.asarray(inputs["L0_attn_W_output"], np.float32).T),
        "par": _pack_par(inputs["L0_attn_time_first"], inputs["L0_attn_time_decay"],
                         np.asarray(inputs["L0_attn_time_mix_k"]).ravel(),
                         np.asarray(inputs["L0_attn_time_mix_v"]).ravel(),
                         np.asarray(inputs["L0_attn_time_mix_r"]).ravel()),
        "idm": np.eye(128, dtype=np.float32),
        "gw": np.ascontiguousarray(np.asarray(inputs["L0_ffn_expert_gate_w"], np.float32)),
        "uw": np.ascontiguousarray(np.asarray(inputs["L0_ffn_expert_up_w"], np.float32)),
        "dw": np.ascontiguousarray(np.asarray(inputs["L0_ffn_expert_down_w"], np.float32)),
        "cw": np.ascontiguousarray(np.asarray(inputs["L0_ffn_conv_w"], np.float32).transpose(2, 1, 0)),
        "su": np.ascontiguousarray(np.asarray(inputs["L0_ffn_shared_up_w"], np.float32)),
        "sd": np.ascontiguousarray(np.asarray(inputs["L0_ffn_shared_down_w"], np.float32)),
        "rw": np.ascontiguousarray(np.asarray(inputs["L0_ffn_router_w"], np.float32)),
        "rb": np.ascontiguousarray(np.asarray(inputs["L0_ffn_router_bias"], np.float32).reshape(16, 1)),
    }
    x = np.asarray(inputs["L0_attn_x"], np.float32)
    return [dict(shared, x=np.ascontiguousarray(x[b].T)) for b in range(B)]


def kernel(**inputs):
    nc = build()
    in_maps = make_in_maps(inputs)
    res = run_bass_kernel_spmd(nc, in_maps, core_ids=list(range(B)))
    return np.ascontiguousarray(
        np.stack([res.results[b]["out"].T for b in range(B)], axis=0)).astype(np.float32)
